# version 08
# baseline (speedup 1.0000x reference)
import numpy as np
import ml_dtypes

import concourse.bass as bass
import concourse.mybir as mybir
from concourse.bass_utils import run_bass_kernel_spmd

F32 = mybir.dt.float32
BF16 = mybir.dt.bfloat16
I32 = mybir.dt.int32
AF = mybir.ActivationFunctionType
ALU = mybir.AluOpType
AX = mybir.AxisListType

NCORES = 8
TOK = 1024
NTT = 4
CG = 4
NCG = 32 // CG
D = 64
EPS = 1e-5
INV_D = 1.0 / 2048.0

SYNC, PE, VEC, SCA, POOL = "sync", "tensor", "vector", "scalar", "gpsimd"


def build_program():
    nc = bass.Bass("TRN2", target_bir_lowering=False, debug=False, detect_race_conditions=False)

    d_x = nc.dram_tensor("x", [TOK, 2048], F32, kind="ExternalInput").ap()
    d_y = nc.dram_tensor("y", [TOK, 2048], F32, kind="ExternalOutput").ap()
    d_r1 = nc.dram_tensor("r1_scratch", [TOK, 2048], F32, kind="Internal").ap()
    d_wqkv = nc.dram_tensor("wqkv_p", [128, 3072], F32, kind="ExternalInput").ap()
    d_rows = nc.dram_tensor("rows_f32", [1, 6400], F32, kind="ExternalInput").ap()
    d_wout = nc.dram_tensor("wout_bd", [128, 128], F32, kind="ExternalInput").ap()
    d_ident = nc.dram_tensor("identity", [128, 128], BF16, kind="ExternalInput").ap()
    d_scales = nc.dram_tensor("scales", [128, 144], F32, kind="ExternalInput").ap()
    d_wg = nc.dram_tensor("wg_s", [64, 128, 2048], I32, kind="ExternalInput").ap()
    d_wu = nc.dram_tensor("wu_s", [64, 128, 2048], I32, kind="ExternalInput").ap()
    d_wd = nc.dram_tensor("wd_s", [16, 128, 8192], I32, kind="ExternalInput").ap()

    A = lambda name, shape, dt: nc.alloc_sbuf_tensor(name, shape, dt)

    wqkv_sb = A("wqkv_sb", [128, 3072], BF16).ap()
    woutbd = A("woutbd", [128, 128], BF16).ap()
    ident = A("ident_sb", [128, 128], BF16).ap()
    scales = A("scales_sb", [128, 272], F32).ap()
    rows = A("rows_sb", [1, 6400], BF16).ap()
    stats = A("stats_sb", [128, 16], F32).ap()
    cns = A("cns_sb", [128, 6], F32).ap()
    zf = A("zf_sb", [128, CG * D], F32).ap()
    tm_scr = A("tm_scr", [128, 2048], BF16).ap()
    pair_scr = A("pair_scr", [128, 2048], BF16).ap()
    scr1 = A("scr1", [128, 1024], BF16).ap()
    scr2 = A("scr2", [128, 1024], BF16).ap()
    dsb = A("dsb", [128, 2 * 512], BF16).ap()
    wstage = A("wstage", [128, 4096], BF16).ap()
    qkv_blob = A("qkv_blob", [128, 6144], BF16).ap()
    eblob = A("eblob", [128, CG * 2048], BF16).ap()
    ctx_f = A("ctx_f", [128, 2048], F32).ap()
    zscr = A("zscr", [128, CG * 1024], BF16).ap()
    jit1 = A("jit1", [128, 2048], BF16).ap()
    jit2 = A("jit2", [128, 2048], BF16).ap()
    xf2 = A("xf2", [128, 2 * 8192], BF16).ap()
    r_base = [None, None]
    r_base[0] = nc.sbuf_base
    r_e = A("r_even", [128, 4096], F32).ap()
    r_base[1] = nc.sbuf_base
    r_o = A("r_odd", [128, 4096], F32).ap()
    r_regions = [r_e, r_o]
    hh_base = nc.sbuf_base
    h_high = A("h_high", [128, 48 * 512], BF16).ap()
    h_low = [nc.alloc_sbuf_tensor_at("h_low_e", [128, 8192], BF16, offset=r_base[0]).ap(),
             nc.alloc_sbuf_tensor_at("h_low_o", [128, 8192], BF16, offset=r_base[1]).ap()]
    cst1 = nc.alloc_sbuf_tensor_at("cst1", [128, 3072], F32, offset=hh_base).ap()
    cst2 = nc.alloc_sbuf_tensor_at("cst2", [1, 6400], F32, offset=hh_base + 12288).ap()
    cst3 = nc.alloc_sbuf_tensor_at("cst3", [128, 128], F32, offset=hh_base + 12288 + 25600).ap()

    def h_ap(h, j):
        if j < 16:
            return h_low[h % 2][:, j * 512:(j + 1) * 512]
        return h_high[:, (j - 16) * 512:(j - 15) * 512]

    def xffnT(h):
        return xf2[:, (h % 2) * 8192:(h % 2) * 8192 + 8192]

    downT = xffnT

    ps_g = nc.alloc_psum_tensor("ps_g", [128, 1024], F32).ap()
    ps_u = nc.alloc_psum_tensor("ps_u", [128, 1024], F32).ap()
    ps_m = nc.alloc_psum_tensor("ps_m", [128, 1024], F32).ap()
    ps_tr = nc.alloc_psum_tensor("ps_tr", [128, 2048], BF16).ap()
    ps_trf = ps_tr.bitcast(F32)

    class Ctr:
        def __init__(self, name):
            self.h = nc.alloc_semaphore(name)
            self.n = 0

    s_pe, s_ve, s_se = Ctr("s_pe"), Ctr("s_ve"), Ctr("s_se")
    s_dc = Ctr("s_dc")
    s_dx = [Ctr(f"s_dx{i}") for i in range(4)]
    s_do = [Ctr(f"s_do{i}") for i in range(4)]
    s_rw = [Ctr(f"s_rw{i}") for i in range(4)]
    s_rb = Ctr("s_rb")
    s_wg = [Ctr(f"s_wg{i}") for i in range(2)]
    s_wu = [Ctr(f"s_wu{i}") for i in range(2)]
    s_wd = [Ctr(f"s_wd{i}") for i in range(4)]

    prog = {k: [] for k in (SYNC, PE, VEC, SCA, POOL)}

    def em(eng, fn):
        prog[eng].append(fn)

    def wait(eng, ctr, val):
        if val > 0:
            em(eng, lambda e, hh=ctr.h, v=val: e.wait_ge(hh, v))

    def pe_inc(fn):
        s_pe.n += 1
        em(PE, lambda e, f=fn: f(e).then_inc(s_pe.h, 1))
        return s_pe.n

    def ve_inc(fn):
        s_ve.n += 1
        em(VEC, lambda e, f=fn: f(e).then_inc(s_ve.h, 1))
        return s_ve.n

    def se_inc(fn):
        s_se.n += 1
        em(SCA, lambda e, f=fn: f(e).then_inc(s_se.h, 1))
        return s_se.n

    def vsync():
        wait(VEC, s_ve, s_ve.n)

    def ssync():
        wait(SCA, s_se, s_se.n)

    def sync_dma(ctr, dst, src):
        ctr.n += 16
        em(SYNC, lambda e, d=dst, s=src: e.dma_start(out=d, in_=s).then_inc(ctr.h, 16))
        return ctr.n

    def pool_dma(ctr, dst, src):
        ctr.n += 16
        em(POOL, lambda e, d=dst, s=src: e.dma_start(out=d, in_=s).then_inc(ctr.h, 16))
        return ctr.n

    cv = sync_dma(s_dc, cst1[:], d_wqkv[:])
    cv = sync_dma(s_dc, cst2[:], d_rows[:])
    cv = sync_dma(s_dc, cst3[:], d_wout[:])
    cv = sync_dma(s_dc, ident[:], d_ident[:])
    cv = sync_dma(s_dc, scales[:, 0:144], d_scales[:])
    wait(VEC, s_dc, cv)
    ve_inc(lambda e: e.tensor_copy(wqkv_sb[:], cst1[:]))
    ve_inc(lambda e: e.tensor_copy(rows[:], cst2[:]))
    ve_inc(lambda e: e.tensor_copy(woutbd[:], cst3[:]))
    ve_inc(lambda e: e.tensor_scalar(scales[:, 144:208], scales[:, 0:64], -1.0, None, ALU.mult))
    ve_inc(lambda e: e.tensor_scalar(scales[:, 208:272], scales[:, 0:64], 0.5, None, ALU.mult))
    ve_inc(lambda e: e.memset(cns[:, 0:1], -0.5))
    ve_inc(lambda e: e.memset(cns[:, 1:2], -1.0))
    ve_inc(lambda e: e.memset(cns[:, 2:3], -128.0))
    ve_inc(lambda e: e.memset(cns[:, 3:4], 1.0))
    consts_done = ve_inc(lambda e: e.memset(cns[:, 4:5], 0.125))
    for eng in (PE, SCA, POOL):
        wait(eng, s_ve, consts_done)

    bqkv_row = rows[:, 0:6144]
    bout_row = rows[:, 6144:6272]
    ones_row = rows[:, 6272:6400]

    gsc = lambda j: scales[:, j:j + 1]
    usc = lambda j: scales[:, 64 + j:64 + j + 1]
    dsc = lambda j: scales[:, 128 + j:128 + j + 1]
    hgsc = lambda j: scales[:, 208 + j:208 + j + 1]

    sq_col = stats[:, 0:1]
    sum_col = stats[:, 1:2]
    negm_col = stats[:, 2:3]
    t2_col = stats[:, 3:4]
    var_col = stats[:, 4:5]
    rstd_col = stats[:, 5:6]

    pair_guard = [0]

    def emit_ln_stats(src_ap, sca_wait=None):
        if sca_wait is not None:
            wait(SCA, sca_wait[0], sca_wait[1])
        wait(SCA, s_ve, pair_guard[0])
        v1 = se_inc(lambda e, s=src_ap: e.activation(pair_scr[:], s, AF.Square, accum_out=sq_col))
        ve_inc(lambda e, s=src_ap: e.reduce_sum(sum_col, s, axis=AX.X))
        wait(VEC, s_se, v1)
        vsync()
        ve_inc(lambda e: e.tensor_scalar(negm_col, sum_col, -INV_D, None, ALU.mult))
        vsync()
        ve_inc(lambda e: e.tensor_tensor(t2_col, negm_col, negm_col, ALU.mult))
        ve_inc(lambda e: e.tensor_scalar(var_col, sq_col, INV_D, EPS, ALU.mult, ALU.add))
        vsync()
        v2 = ve_inc(lambda e: e.tensor_tensor(var_col, var_col, t2_col, ALU.subtract))
        wait(SCA, s_ve, v2)
        se_inc(lambda e: e.activation(var_col, var_col, AF.Ln))
        ssync()
        return se_inc(lambda e: e.activation(rstd_col, var_col, AF.Exp, scale=cns[:, 0:1]))

    trb_state = [(s_se, 0), (s_se, 0)]
    tr_next = [0]
    psg_state = [(s_ve, 0), (s_ve, 0)]
    psu_state = [(s_ve, 0), (s_ve, 0)]
    psm_state = [(s_ve, 0), (s_ve, 0)]
    BANKS = {"tr": (trb_state, None), "g": (psg_state, None), "u": (psu_state, None)}

    def emit_transposes(groups, first_wait=None):
        last = 0
        for srcs, dst_emit in groups:
            slot = tr_next[0]
            tr_next[0] ^= 1
            if first_wait is not None:
                wait(PE, first_wait[0], first_wait[1])
                first_wait = None
            wait(PE, trb_state[slot][0], trb_state[slot][1])
            for k, src in enumerate(srcs):
                dstp = ps_tr[:, slot * 1024 + k * 128: slot * 1024 + (k + 1) * 128]
                if k == 3:
                    pe_inc(lambda e, d=dstp, s=src: e.transpose(d, s, ident[:]))
                else:
                    em(PE, lambda e, d=dstp, s=src: e.transpose(d, s, ident[:]))
            wait(SCA, s_pe, s_pe.n)
            slot_ap = ps_tr[:, slot * 1024: slot * 1024 + 512]
            last = se_inc(lambda e, sa=slot_ap, de=dst_emit: de(e, sa))
            trb_state[slot] = (s_se, last)
        return last

    def transpose16(src_sb, dst_for_group, first_wait=None):
        groups = []
        for grp in range(4):
            srcs = [src_sb[:, (grp * 4 + k) * 128:(grp * 4 + k + 1) * 128] for k in range(4)]
            groups.append((srcs, dst_for_group(grp)))
        return emit_transposes(groups, first_wait=first_wait)

    def mk_dst_plain(dst_base):
        def f(grp):
            def emitfn(e, slot_ap, g=grp, db=dst_base):
                return e.copy(db[:, g * 512:(g + 1) * 512], slot_ap)
            return emitfn
        return f

    last_store = [None]

    def attn_tile(h, tt, ebufs, qkv_banks, op_banks):
        g_tt = h * NTT + tt
        slot = tt % 2
        r_tt = r_regions[h % 2][:, slot * 2048:(slot + 1) * 2048]
        xs = s_dx[(h % 2) * 2 + slot]
        rws = s_rw[(h % 2) * 2 + slot]
        if tt >= 2:
            wait(SYNC, rws, rws.n)
        xv = sync_dma(xs, r_tt, d_x[g_tt * 128:(g_tt + 1) * 128, :])
        wait(SCA, xs, xv)
        wait(VEC, xs, xv)
        yield "ld"

        rs1 = emit_ln_stats(r_tt)
        wait(VEC, s_se, rs1)
        ln1_done = ve_inc(lambda e, r=r_tt: e.tensor_scalar(
            tm_scr[:], r, negm_col, rstd_col, ALU.add, ALU.mult))
        yield "ln1"

        ln1T = jit1
        ev_ln1T = transpose16(tm_scr, mk_dst_plain(ln1T), first_wait=(s_ve, ln1_done))
        yield "ln1T"

        bank_state = BANKS[qkv_banks][0]
        bank_psum = {"tr": ps_trf, "g": ps_g, "u": ps_u}[qkv_banks]
        for p in range(16):
            bank = p % 2
            wait(PE, bank_state[bank][0], bank_state[bank][1])
            if p == 0:
                wait(PE, s_se, ev_ln1T)
            dst0 = bank_psum[:, bank * 512: bank * 512 + 192]
            dst1 = bank_psum[:, bank * 512 + 192: bank * 512 + 384]
            em(PE, lambda e, d=dst0, pp=p: e.matmul(
                d, ln1T[0:64, pp * 128:(pp + 1) * 128], wqkv_sb[0:64, pp * 192:(pp + 1) * 192],
                start=True, stop=False))
            em(PE, lambda e, d=dst0, pp=p: e.matmul(
                d, ones_row, bqkv_row[:, pp * 384: pp * 384 + 192], start=False, stop=True))
            em(PE, lambda e, d=dst1, pp=p: e.matmul(
                d, ln1T[64:128, pp * 128:(pp + 1) * 128], wqkv_sb[64:128, pp * 192:(pp + 1) * 192],
                start=True, stop=False))
            pv = pe_inc(lambda e, d=dst1, pp=p: e.matmul(
                d, ones_row, bqkv_row[:, pp * 384 + 192: pp * 384 + 384], start=False, stop=True))
            wait(VEC, s_pe, pv)
            evn = ve_inc(lambda e, pp=p, b=bank, bp=bank_psum: e.tensor_copy(
                qkv_blob[:, pp * 384:(pp + 1) * 384], bp[:, b * 512: b * 512 + 384]))
            bank_state[bank] = (s_ve, evn)
            if p % 4 == 3:
                yield f"qkv{p}"

        blob = qkv_blob.rearrange("p (c three d) -> p c three d", c=32, three=3)
        e4s = [b.rearrange("p (c e d) -> p c e d", c=CG, e=32) for b in ebufs]
        nb = len(ebufs)
        zs4 = zscr.rearrange("p (c e d) -> p c e d", c=CG, e=16)
        zf4 = zf.rearrange("p (c o d) -> p c o d", c=CG, o=1)
        zfc = zf.rearrange("p (c d) -> p c d", c=CG)
        ctx4 = ctx_f.rearrange("p (o e d) -> p o e d", o=1, e=32)
        p2 = pair_scr.rearrange("p (o e d) -> p o e d", o=1, e=32)
        kview = blob[:, :, 1, :].unsqueeze(1).broadcast_to([128, CG, 32, D])

        prod_c = [0] * NCG
        exp_c = [0] * NCG
        rz_c = [0] * NCG

        def emit_product(g):
            e4 = e4s[g % nb]
            qv = blob[:, g * CG:(g + 1) * CG, 0, :].unsqueeze(2).broadcast_to([128, CG, 32, D])
            prod_c[g] = ve_inc(lambda e, t=e4, q=qv: e.tensor_tensor(t, q, kview, ALU.mult))
            wait(SCA, s_ve, prod_c[g])
            fl = ebufs[g % nb]
            exp_c[g] = se_inc(lambda e, f=fl: e.activation(f[:], f[:], AF.Exp, scale=cns[:, 4:5]))

        def emit_midA(g):
            e4 = e4s[g % nb]
            wait(VEC, s_se, exp_c[g])
            em(VEC, lambda e, t=e4: e.tensor_tensor(zs4, t[:, :, 0:16, :], t[:, :, 16:32, :], ALU.add))
            em(VEC, lambda e: e.tensor_tensor(zs4[:, :, 0:8, :], zs4[:, :, 0:8, :], zs4[:, :, 8:16, :], ALU.add))
            ve_inc(lambda e: e.tensor_tensor(zs4[:, :, 0:4, :], zs4[:, :, 0:4, :], zs4[:, :, 4:8, :], ALU.add))
            vsync()
            ve_inc(lambda e: e.tensor_tensor(zs4[:, :, 0:2, :], zs4[:, :, 0:2, :], zs4[:, :, 2:4, :], ALU.add))
            vsync()
            zv = ve_inc(lambda e: e.tensor_tensor(zf4, zs4[:, :, 0:1, :], zs4[:, :, 1:2, :], ALU.add))
            wait(SCA, s_ve, zv)
            se_inc(lambda e: e.activation(zf[:], zf[:], AF.Ln))
            ssync()
            rz_c[g] = se_inc(lambda e: e.activation(zf[:], zf[:], AF.Exp, scale=cns[:, 1:2]))

        def emit_midB(g):
            e4 = e4s[g % nb]
            wait(VEC, s_se, rz_c[g])
            vsl = blob[:, g * CG:(g + 1) * CG, 2, :]
            ve_inc(lambda e, v=vsl: e.tensor_tensor(v, v, zfc, ALU.mult))
            vsync()
            wb = vsl.unsqueeze(2).broadcast_to([128, CG, 32, D])
            em(VEC, lambda e, t=e4, w=wb: e.tensor_tensor(t, w, t, ALU.mult))
            em(VEC, lambda e, t=e4: e.tensor_tensor(t[:, 0:2], t[:, 0:2], t[:, 2:4], ALU.add))
            if g == 0:
                ve_inc(lambda e, t=e4: e.tensor_tensor(ctx4, t[:, 0:1], t[:, 1:2], ALU.add))
            else:
                em(VEC, lambda e, t=e4: e.tensor_tensor(p2, t[:, 0:1], t[:, 1:2], ALU.add))
                pair_guard[0] = ve_inc(lambda e: e.tensor_tensor(ctx_f[:], ctx_f[:], pair_scr[:], ALU.add))

        if nb >= 3:
            emit_product(0)
            emit_product(1)
            yield "p01"
            emit_midA(0)
            for g in range(1, NCG):
                if g + 1 < NCG:
                    emit_product(g + 1)
                yield f"pa{g}"
                emit_midB(g - 1)
                emit_midA(g)
                yield f"mid{g}"
            emit_midB(NCG - 1)
        elif nb == 2:
            emit_product(0)
            emit_product(1)
            yield "p01"
            emit_midA(0)
            for g in range(1, NCG):
                emit_midB(g - 1)
                if g + 1 < NCG:
                    emit_product(g + 1)
                yield f"pb{g}"
                emit_midA(g)
                yield f"pa{g}"
            emit_midB(NCG - 1)
        else:
            for g in range(NCG):
                emit_product(g)
                yield f"p{g}"
                emit_midA(g)
                yield f"a{g}"
                emit_midB(g)
                yield f"b{g}"

        ctx_bf = jit1
        cvv = ve_inc(lambda e: e.tensor_copy(ctx_bf[:], ctx_f[:]))
        ctxT = jit2
        ev_ctxT = transpose16(ctx_bf, mk_dst_plain(ctxT), first_wait=(s_ve, cvv))
        yield "ctxT"

        ob_state = BANKS[op_banks][0]
        ob_psum = {"tr": ps_trf, "g": ps_g, "u": ps_u}[op_banks]
        attnout = tm_scr
        op_evacs = []
        for mg in range(4):
            bank = mg % 2
            wait(PE, ob_state[bank][0], ob_state[bank][1])
            if mg == 0:
                wait(PE, s_se, ev_ctxT)
            for k in range(4):
                m2 = mg * 4 + k
                dstp = ob_psum[:, bank * 512 + k * 128: bank * 512 + (k + 1) * 128]
                em(PE, lambda e, d=dstp, mm=m2: e.matmul(
                    d, ctxT[:, mm * 128:(mm + 1) * 128], woutbd[:], start=True, stop=False))
                if k == 3:
                    pe_inc(lambda e, d=dstp: e.matmul(d, ones_row, bout_row, start=False, stop=True))
                else:
                    em(PE, lambda e, d=dstp: e.matmul(d, ones_row, bout_row, start=False, stop=True))
            wait(SCA, s_pe, s_pe.n)
            evn = se_inc(lambda e, b=bank, m=mg, bp=ob_psum: e.copy(
                attnout[:, m * 512:(m + 1) * 512], bp[:, b * 512:(b + 1) * 512]))
            ob_state[bank] = (s_se, evn)
            op_evacs.append(evn)
        yield "oproj"

        wait(VEC, s_se, op_evacs[-1])
        rs2 = emit_ln_stats(attnout[:])
        wait(VEC, s_se, rs2)
        ve_inc(lambda e: e.tensor_scalar(pair_scr[:], attnout[:], negm_col, rstd_col,
                                         ALU.add, ALU.mult))
        vsync()
        r1v = ve_inc(lambda e, r=r_tt: e.tensor_tensor(r, r, pair_scr[:], ALU.add))
        pair_guard[0] = r1v
        yield "ln2"

        rs3 = emit_ln_stats(r_tt, sca_wait=(s_ve, r1v))
        wait(VEC, s_se, rs3)
        ln3_done = ve_inc(lambda e, r=r_tt: e.tensor_scalar(
            tm_scr[:], r, negm_col, rstd_col, ALU.add, ALU.mult))
        xfv = xffnT(h).rearrange("p (f t) -> p f t", f=16)

        def xf_dst(grp):
            def emitfn(e, slot_ap, g=grp, t0=tt):
                return e.copy(xfv[:, g * 4:(g + 1) * 4, t0 * 128:(t0 + 1) * 128],
                              slot_ap.rearrange("p (b f) -> p b f", b=4))
            return emitfn

        transpose16(tm_scr, xf_dst, first_wait=(s_ve, ln3_done))
        yield "ln3"

        wait(SYNC, s_ve, r1v)
        sync_dma(rws, d_r1[g_tt * 128:(g_tt + 1) * 128, :], r_tt)
        yield "r1out"

    def ln4_tile(h, tt, dn_ready, ctx_guard):
        g_tt = h * NTT + tt
        dT_tt = downT(h)[:, tt * 2048:(tt + 1) * 2048]
        rws = s_rw[(h % 2) * 2 + tt % 2]
        wait(SYNC, rws, rws.n)
        if ctx_guard is not None:
            wait(SYNC, ctx_guard[0], ctx_guard[1])
        if last_store[0] is not None:
            wait(SYNC, last_store[0][0], last_store[0][1])
        rb = sync_dma(s_rb, ctx_f[:], d_r1[g_tt * 128:(g_tt + 1) * 128, :])
        yield "rb"
        wait(SCA, s_se, dn_ready)
        wait(VEC, s_se, dn_ready)
        rs4 = emit_ln_stats(dT_tt)
        wait(VEC, s_se, rs4)
        ve_inc(lambda e, d=dT_tt: e.tensor_scalar(pair_scr[:], d, negm_col, rstd_col,
                                                  ALU.add, ALU.mult))
        vsync()
        wait(VEC, s_rb, rb)
        rv = ve_inc(lambda e: e.tensor_tensor(ctx_f[:], ctx_f[:], pair_scr[:], ALU.add))
        pair_guard[0] = rv
        wait(SYNC, s_ve, rv)
        sd = sync_dma(s_do[tt % 4], d_y[g_tt * 128:(g_tt + 1) * 128, :], ctx_f[:])
        last_store[0] = (s_do[tt % 4], sd)
        yield "store"

    def ffn_half(h, pump, stage_guard):
        conv_g = [[0, 0] for _ in range(64)]
        conv_u = [[0, 0] for _ in range(64)]
        pe_g = [[0, 0] for _ in range(64)]
        pe_u = [[0, 0] for _ in range(64)]
        hdone = [0] * 64
        rw_a = s_rw[(h % 2) * 2 + 0].n
        rw_b = s_rw[(h % 2) * 2 + 1].n

        def gate_dma(j, half):
            slotn = half
            gslot = wstage[:, slotn * 1024: slotn * 1024 + 1024]
            if j >= 1:
                wait(POOL, s_pe, pe_g[j - 1][half])
            elif stage_guard:
                wait(POOL, s_pe, stage_guard)
            gv = pool_dma(s_wg[half], gslot, d_wg[j][:, half * 1024:(half + 1) * 1024])
            wait(SCA, s_wg[half], gv)
            conv_g[j][half] = se_inc(lambda e, d=gslot: e.activation(d, d, AF.Identity, bias=cns[:, 2:3]))

        def up_dma(j, half):
            slotn = 2 + half
            uslot = wstage[:, slotn * 1024: slotn * 1024 + 1024]
            if j >= 1:
                wait(POOL, s_pe, pe_u[j - 1][half])
            elif stage_guard:
                wait(POOL, s_pe, stage_guard)
            uv = pool_dma(s_wu[half], uslot, d_wu[j][:, half * 1024:(half + 1) * 1024])
            wait(SCA, s_wu[half], uv)
            conv_u[j][half] = se_inc(lambda e, d=uslot: e.activation(d, d, AF.Identity, bias=cns[:, 2:3]))

        gate_dma(0, 0); gate_dma(0, 1); up_dma(0, 0); up_dma(0, 1)
        for j in range(64):
            gb = ps_g[:, (j % 2) * 512:(j % 2) * 512 + 512]
            ub = ps_u[:, (j % 2) * 512:(j % 2) * 512 + 512]
            wait(PE, psg_state[j % 2][0], psg_state[j % 2][1])
            for half in range(2):
                wait(PE, s_se, conv_g[j][half])
                for f2 in range(8):
                    f = half * 8 + f2
                    lhs = wstage[:, half * 1024 + f2 * 128: half * 1024 + (f2 + 1) * 128]
                    rhs = xffnT(h)[:, f * 512:(f + 1) * 512]
                    if f2 == 7:
                        pe_g[j][half] = pe_inc(lambda e, d=gb, l=lhs, r=rhs, fin=(f == 15): e.matmul(
                            d, l, r, start=False, stop=fin))
                    else:
                        em(PE, lambda e, d=gb, l=lhs, r=rhs, st=(f == 0): e.matmul(
                            d, l, r, start=st, stop=False))
            wait(PE, psu_state[j % 2][0], psu_state[j % 2][1])
            for half in range(2):
                wait(PE, s_se, conv_u[j][half])
                for f2 in range(8):
                    f = half * 8 + f2
                    lhs = wstage[:, (2 + half) * 1024 + f2 * 128: (2 + half) * 1024 + (f2 + 1) * 128]
                    rhs = xffnT(h)[:, f * 512:(f + 1) * 512]
                    if f2 == 7:
                        pe_u[j][half] = pe_inc(lambda e, d=ub, l=lhs, r=rhs, fin=(f == 15): e.matmul(
                            d, l, r, start=False, stop=fin))
                    else:
                        em(PE, lambda e, d=ub, l=lhs, r=rhs, st=(f == 0): e.matmul(
                            d, l, r, start=st, stop=False))
            if j + 1 < 64:
                gate_dma(j + 1, 0); gate_dma(j + 1, 1)
                up_dma(j + 1, 0); up_dma(j + 1, 1)

            sp = (j % 2) * 512
            wait(SCA, s_pe, pe_g[j][1])
            th_v = se_inc(lambda e, g=gb, jj=j, s=sp: e.activation(
                scr1[:, s:s + 512], g, AF.Tanh, scale=hgsc(jj)))
            wait(VEC, s_se, th_v)
            wait(VEC, s_pe, pe_u[j][1])
            ve_inc(lambda e, s=sp: e.tensor_scalar(
                scr2[:, s:s + 512], scr1[:, s:s + 512], 1.0, None, ALU.add))
            vsync()
            gu = ve_inc(lambda e, g=gb, jj=j, s=sp: e.scalar_tensor_tensor(
                scr1[:, s:s + 512], g, hgsc(jj), scr2[:, s:s + 512], ALU.mult, ALU.mult))
            psg_state[j % 2] = (s_ve, gu)
            vsync()
            if j == 0:
                wait(VEC, s_rw[(h % 2) * 2 + 0], rw_a)
                wait(VEC, s_rw[(h % 2) * 2 + 1], rw_b)
            hdone[j] = ve_inc(lambda e, u=ub, jj=j, s=sp: e.scalar_tensor_tensor(
                h_ap(h, jj), u, usc(jj), scr1[:, s:s + 512], ALU.mult, ALU.mult))
            psu_state[j % 2] = (s_ve, hdone[j])
            pump(1)

        wait(SCA, s_pe, pe_u[63][1])

        NE = 128
        conv_d = [0] * NE
        pe_e8 = [0] * NE
        dvac = [0] * 16
        dTv = downT(h).rearrange("p (t f) -> p t f", t=NTT)

        def down_dma(e8):
            hh, k = e8 // 8, e8 % 8
            slotn = e8 % 4
            slt = wstage[:, slotn * 1024: slotn * 1024 + 1024]
            if e8 < 4:
                guards = [pe_g[63][0], pe_g[63][1], pe_u[63][0], pe_u[63][1]]
                wait(POOL, s_pe, guards[e8])
            else:
                wait(POOL, s_pe, pe_e8[e8 - 4])
            dv = pool_dma(s_wd[slotn], slt, d_wd[hh][:, k * 1024:(k + 1) * 1024])
            wait(SCA, s_wd[slotn], dv)
            conv_d[e8] = se_inc(lambda e, d=slt: e.activation(d, d, AF.Identity, bias=cns[:, 2:3]))

        down_dma(0); down_dma(1); down_dma(2); down_dma(3)
        dn_evac = [0] * 16
        for hh in range(16):
            bank = hh % 2
            db = ps_m[:, bank * 512: bank * 512 + 512]
            wait(PE, psm_state[bank][0], psm_state[bank][1])
            if hh == 0:
                wait(PE, s_ve, hdone[63])
            for k in range(8):
                e8 = hh * 8 + k
                wait(PE, s_se, conv_d[e8])
                for i2 in range(8):
                    i = k * 8 + i2
                    lhs = wstage[:, (e8 % 4) * 1024 + i2 * 128: (e8 % 4) * 1024 + (i2 + 1) * 128]
                    rhs = h_ap(h, i)
                    if i2 == 7:
                        pe_e8[e8] = pe_inc(lambda e, d=db, l=lhs, r=rhs, fin=(i == 63): e.matmul(
                            d, l, r, start=False, stop=fin))
                    else:
                        em(PE, lambda e, d=db, l=lhs, r=rhs, st=(i == 0): e.matmul(
                            d, l, r, start=st, stop=False))
                if e8 + 4 < NE:
                    down_dma(e8 + 4)
                if k % 4 == 3:
                    pump(1)
            wait(VEC, s_pe, pe_e8[hh * 8 + 7])
            dslot = dsb[:, (hh % 2) * 512:(hh % 2) * 512 + 512]
            dvac[hh] = ve_inc(lambda e, d=dslot, s=db, jj=hh: e.tensor_scalar(
                d, s, dsc(jj), None, ALU.mult))
            psm_state[bank] = (s_ve, dvac[hh])

            def dn_dst(jj):
                def emitfn(e, slot_ap, j2=jj):
                    return e.copy(dTv[:, :, j2 * 128:(j2 + 1) * 128],
                                  slot_ap.rearrange("p (b f) -> p b f", b=4))
                return emitfn

            srcs = [dslot[:, b * 128:(b + 1) * 128] for b in range(4)]
            dn_evac[hh] = emit_transposes([(srcs, dn_dst(hh))], first_wait=(s_ve, dvac[hh]))
        return dn_evac[15], s_pe.n

    eb_soloA = xf2[:, 8192:8192 + CG * 2048]

    for tt in range(NTT):
        for _ in attn_tile(0, tt, [eblob, eb_soloA], qkv_banks="g", op_banks="u"):
            pass

    gen_queue = [attn_tile(1, tt, [eblob], qkv_banks="tr", op_banks="tr") for tt in range(NTT)]

    def pump_b(n):
        for _ in range(n):
            while gen_queue:
                try:
                    next(gen_queue[0])
                    break
                except StopIteration:
                    gen_queue.pop(0)
            if not gen_queue:
                return

    dnA, peA = ffn_half(0, pump_b, stage_guard=0)
    pump_b(10 ** 6)
    ctxA_guard = (s_ve, s_ve.n)

    ln4_queue = [ln4_tile(0, tt, dnA, ctxA_guard if tt == 0 else None) for tt in range(NTT)]
    cnt = [0]

    def pump_ln4(n):
        cnt[0] += n
        while cnt[0] >= 10 and ln4_queue:
            cnt[0] -= 10
            while ln4_queue:
                try:
                    next(ln4_queue[0])
                    break
                except StopIteration:
                    ln4_queue.pop(0)

    dnB, peB = ffn_half(1, pump_ln4, stage_guard=peA)
    while ln4_queue:
        try:
            next(ln4_queue[0])
        except StopIteration:
            ln4_queue.pop(0)

    for tt in range(NTT):
        for _ in ln4_tile(1, tt, dnB, None):
            pass

    for i in range(4):
        wait(SYNC, s_do[i], s_do[i].n)

    with nc.Block() as block:
        @block.sync
        def _(eng):
            for fn in prog[SYNC]:
                fn(eng)

        @block.tensor
        def _(eng):
            for fn in prog[PE]:
                fn(eng)

        @block.vector
        def _(eng):
            for fn in prog[VEC]:
                fn(eng)

        @block.scalar
        def _(eng):
            for fn in prog[SCA]:
                fn(eng)

        @block.gpsimd
        def _(eng):
            for fn in prog[POOL]:
                fn(eng)

    return nc



def _prep_weights(Wqkv, bqkv, Wout, bout, gate_w, gate_scale, up_w, up_scale,
                  down_w, down_scale):
    f32 = np.float32
    wqkv = np.asarray(Wqkv, dtype=f32)
    wqkv_p = np.ascontiguousarray(
        wqkv.reshape(16, 2, 64, 192).transpose(1, 2, 0, 3).reshape(128, 3072))

    rows = np.zeros((1, 6400), dtype=f32)
    rows[0, 0:6144] = np.asarray(bqkv, dtype=f32).reshape(-1)
    bo = np.asarray(bout, dtype=f32).reshape(-1)
    rows[0, 6144:6208] = bo
    rows[0, 6208:6272] = bo
    rows[0, 6272:6400] = 1.0

    wo = np.asarray(Wout, dtype=f32)
    wout_bd = np.zeros((128, 128), dtype=f32)
    wout_bd[0:64, 0:64] = wo
    wout_bd[64:128, 64:128] = wo

    ident = np.eye(128, dtype=f32).astype(ml_dtypes.bfloat16)

    scales = np.zeros((128, 144), dtype=f32)
    scales[:, 0:64] = np.asarray(gate_scale, dtype=f32).reshape(64, 128).T
    scales[:, 64:128] = np.asarray(up_scale, dtype=f32).reshape(64, 128).T
    scales[:, 128:144] = np.asarray(down_scale, dtype=f32).reshape(16, 128).T

    gw = np.asarray(gate_w, dtype=np.int32)
    uw = np.asarray(up_w, dtype=np.int32)
    dw = np.asarray(down_w, dtype=np.int32)
    wg_s = np.ascontiguousarray(gw.reshape(64, 128, 16, 128).transpose(0, 3, 2, 1)).reshape(64, 128, 2048)
    wu_s = np.ascontiguousarray(uw.reshape(64, 128, 16, 128).transpose(0, 3, 2, 1)).reshape(64, 128, 2048)
    wd_s = np.ascontiguousarray(dw.reshape(16, 128, 64, 128).transpose(0, 3, 2, 1)).reshape(16, 128, 8192)

    return dict(wqkv_p=wqkv_p, rows_f32=rows, wout_bd=wout_bd, identity=ident,
                scales=scales, wg_s=wg_s, wu_s=wu_s, wd_s=wd_s)


def kernel(L0_attn_x, L0_attn_Wqkv, L0_attn_bqkv, L0_attn_Wout, L0_attn_bout,
           L0_ffn_gate_w, L0_ffn_gate_scale, L0_ffn_up_w, L0_ffn_up_scale,
           L0_ffn_down_w, L0_ffn_down_scale):
    x = np.ascontiguousarray(np.asarray(L0_attn_x, dtype=np.float32).reshape(8192, 2048))
    shared = _prep_weights(L0_attn_Wqkv, L0_attn_bqkv, L0_attn_Wout, L0_attn_bout,
                           L0_ffn_gate_w, L0_ffn_gate_scale, L0_ffn_up_w, L0_ffn_up_scale,
                           L0_ffn_down_w, L0_ffn_down_scale)
    nc = build_program()
    in_maps = []
    for c in range(NCORES):
        m = dict(shared)
        m["x"] = np.ascontiguousarray(x[c * TOK:(c + 1) * TOK])
        in_maps.append(m)
    res = run_bass_kernel_spmd(nc, in_maps, core_ids=list(range(NCORES)))
    y = np.concatenate([r["y"] for r in res.results], axis=0)
    return np.ascontiguousarray(y.reshape(1, 8192, 2048).astype(np.float32))
